# version 04 (final)
# baseline (speedup 1.0000x reference)
import contextlib
import numpy as np
import concourse.bass as bass
import concourse.mybir as mybir
from concourse.bass_utils import run_bass_kernel_spmd
import ml_dtypes

F32 = mybir.dt.float32
BF16 = mybir.dt.bfloat16
BF = ml_dtypes.bfloat16
AF = mybir.ActivationFunctionType
ALU = mybir.AluOpType
AX = mybir.AxisListType

D, T, H, DH, FF, DI, DS, NE, FE = 1024, 2048, 16, 64, 4096, 2048, 16, 64, 512
NDT, NTT, NJC = D // 128, T // 128, T // 512
ENGINES = ("sync", "gpsimd", "tensor", "vector", "scalar")
_PI = [0, 0, 0, 1, 1, 2]
_PJ = [1, 2, 3, 2, 3, 3]
_J6 = np.array([[0, 0, 0, 0, 0, 1], [0, 0, 0, 0, -1, 0], [0, 0, 0, 1, 0, 0],
                [0, 0, 1, 0, 0, 0], [0, -1, 0, 0, 0, 0], [1, 0, 0, 0, 0, 0]],
               dtype=np.float32)
NEG_BIG = -1.0e30


class B:

    def __init__(self):
        self.ops = {e: [] for e in ENGINES}
        self.semval = {}
        self.free = {}

    def emit(self, eng, fn, waits=(), incs=()):
        incs = [(s, a) for (s, a) in incs]
        for s, a in incs:
            self.semval[s] = self.semval.get(s, 0) + a
        w = []
        seen = {}
        for s, v in waits:
            if v <= 0:
                continue
            if s in seen:
                seen[s] = max(seen[s], v)
            else:
                seen[s] = v
        w = list(seen.items())
        self.ops[eng].append((fn, w, incs))

    def val(self, s):
        return self.semval.get(s, 0)

    def take(self, name):
        return self.free.pop(name, [])

    def rel(self, name, waits):
        self.free[name] = list(waits)

    def barrier(self):
        toks = {}
        for e in ENGINES:
            if e in ("tensor", "vector", "scalar"):
                self.emit(e, lambda eng: eng.drain(), incs=[("bar_" + e, 1)])
            else:
                self.emit(e, lambda eng: eng.nop(), incs=[("bar_" + e, 1)])
            toks[e] = self.val("bar_" + e)
        for e in ENGINES:
            self.emit(e, lambda eng: eng.nop(),
                      waits=[("bar_" + o, toks[o]) for o in ENGINES if o != e])
        self.free.clear()


def build_program(debug=False, probe=False):
    nc = bass.Bass("TRN2", target_bir_lowering=False)
    b = B()
    top = contextlib.ExitStack()

    def din(name, shape, dtype=F32):
        return nc.dram_tensor(name, list(shape), dtype, kind="ExternalInput").ap()

    x_in = din("x", (D, T))
    wqkvT = din("wqkvT", (D, 3 * D))
    bqkv = din("bqkv", (128, 3 * NDT))
    wlinesT = din("wlinesT", (D, 4 * 64))
    bvrow_d = din("bvrow", (1, D))
    woutT_d = din("woutT", (D, D))
    bout0_d = din("bout0", (128, NDT))
    dlog_d = din("dlog", (H, 1))
    bscale_d = din("bscale", (H, 1))
    w1T_d = din("w1T", (D, FF))
    b1_d = din("b1v", (128, FF // 128))
    w2T_d = din("w2T", (FF, D))
    b2_d = din("b2v", (128, NDT))
    winT_d = din("winT", (D, 2 * DI))
    bin_d = din("binv", (128, 2 * DI // 128))
    wBT_d = din("wBT", (DI, DS))
    wCT_d = din("wCT", (DS, DI))
    dvec_d = din("dvec", (128, DI // 128))
    wout1T_d = din("wout1T", (DI, D))
    bout1_d = din("bout1", (128, NDT))
    centT_d = din("centT", (D, NE))
    ebias_d = din("ebias", (1, NE))
    wsup_d = din("wsup", (D, FE))
    wsdn_d = din("wsdn", (FE, D))
    wup_d = din("wup", (NE, D, FE))
    wdn_d = din("wdn", (NE, FE, D))

    ikind = "ExternalOutput" if debug else "Internal"
    out = nc.dram_tensor("out", [D, T], F32, kind="ExternalOutput").ap()
    x_mid = nc.dram_tensor("x_mid", [D, T], F32, kind=ikind).ap()
    s_scr = nc.dram_tensor("s_scr", [D, T], F32, kind=ikind).ap()
    sz_scr = nc.dram_tensor("sz_scr", [DI // 128, 128, T], BF16,
                            kind=ikind).ap()
    xn_scr = s0_scr = satt_scr = bnd_scr = None
    if probe:
        xn_scr = nc.dram_tensor("xn_scr", [128, NDT * T], BF16,
                                kind="ExternalOutput").ap()
        s0_scr = nc.dram_tensor("s0_scr", [D, T], F32,
                                kind="ExternalOutput").ap()
        satt_scr = nc.dram_tensor("satt_scr", [128, 4 * 8 * 512], BF16,
                                  kind="ExternalOutput").ap()
        bnd_scr = nc.dram_tensor("bnd_scr", [16, 4 * 512], BF16,
                                 kind="ExternalOutput").ap()
        knm_scr = nc.dram_tensor("knm_scr", [16, T + 8], F32,
                                 kind="ExternalOutput").ap()
        knm2_scr = nc.dram_tensor("knm2_scr", [16, 1], F32,
                                  kind="ExternalOutput").ap()
        knr_scr = nc.dram_tensor("knr_scr", [16, 1], F32,
                                 kind="ExternalOutput").ap()
        g0_scr = nc.dram_tensor("g0_scr", [D, T], F32,
                                kind="ExternalOutput").ap()
        h_scr = nc.dram_tensor("h_scr", [128, 8 * 8192], BF16,
                               kind="ExternalOutput").ap()
    dbg_list = []

    def dump(key, ap):
        if not debug:
            return
        b.barrier()
        dt_ = ap.dtype
        dd = nc.dram_tensor("dbg_" + key, list(ap.shape), dt_,
                            kind="ExternalOutput").ap()
        b.emit("sync", lambda e, o=dd, i=ap: e.dma_start(out=o[:, :], in_=i),
               incs=[("dbg", 16)])
        b.emit("sync", lambda e: e.nop(), waits=[("dbg", b.val("dbg"))])
        b.barrier()
        dbg_list.append(key)

    def inline(name, arr):
        return nc.inline_tensor(np.ascontiguousarray(arr), name=name).ap()

    ident_h = inline("identfC", np.eye(128, dtype=np.float32))
    c0_h = inline("c0C", (np.arange(512, dtype=np.float32)[None, :]
                          - np.arange(128, dtype=np.float32)[:, None]))
    ones_col_h = inline("ones_colC", np.ones((128, 1), dtype=BF))
    ones_col_f_h = inline("ones_col_fC", np.ones((128, 1), dtype=np.float32))
    ones_row_h = inline("ones_rowC", np.ones((1, 128), dtype=BF))
    ones_row_f_h = inline("ones_row_fC", np.ones((1, 128), dtype=np.float32))
    mean_col_h = inline("mean_colC", np.full((128, 1), 1.0 / D, dtype=np.float32))
    ppi = np.zeros((64, 4 * 128), dtype=BF)
    ppj = np.zeros((64, 4 * 128), dtype=BF)
    for g in range(4):
        for hh in range(4):
            h = 4 * g + hh
            for m in range(6):
                ppi[4 * h + _PI[m], g * 128 + 32 * hh + m] = 1
                ppj[4 * h + _PJ[m], g * 128 + 32 * hh + m] = 1
    ppi_h = inline("perm_piC", ppi)
    ppj_h = inline("perm_pjC", ppj)
    sum6 = np.zeros((128, 4), dtype=BF)
    spread6 = np.zeros((4, 128), dtype=BF)
    j6t4 = np.zeros((128, 128), dtype=BF)
    for hh in range(4):
        sum6[32 * hh:32 * hh + 6, hh] = 1
        spread6[hh, 32 * hh:32 * hh + 6] = 1
        j6t4[32 * hh:32 * hh + 6, 32 * hh:32 * hh + 6] = _J6.astype(BF)
    sum6_h = inline("sum6C", sum6)
    spread6_h = inline("spread6C", spread6)
    j6t4_h = inline("j6t4C", j6t4)
    spreadh = np.zeros((16, 128), dtype=np.float32)
    for h in range(16):
        for m in range(6):
            spreadh[h, 32 * (h % 4) + m] = 1
    spreadh_h = inline("spreadhC", spreadh)
    selp4 = np.zeros((16, 4), dtype=np.float32)
    for h in range(16):
        selp4[h, h // 4] = 1
    selp4_h = inline("selp4C", selp4)
    blkh = np.zeros((128, 8 * 16), dtype=BF)
    for j in range(8):
        blkh[0:64, j * 16 + 2 * j] = 1
        blkh[64:128, j * 16 + 2 * j + 1] = 1
    blkh_h = inline("blkhC", blkh)
    selh = np.zeros((16, 16 * 128), dtype=BF)
    for h in range(16):
        selh[h, h * 128:(h + 1) * 128] = 1
    selh_h = inline("selhC", selh)

    def sbuf_in(stack, name, shape, dtype):
        return stack.enter_context(nc.sbuf_tensor(name, list(shape), dtype)).ap()

    pbanks = {}
    for nm in "ABCDEFGH":
        pbanks[nm] = top.enter_context(
            nc.psum_tensor("p" + nm, [128, 512], F32)).ap()
    pA, pB, pC, pD = pbanks["A"], pbanks["B"], pbanks["C"], pbanks["D"]
    pE, pF, pG, pH = pbanks["E"], pbanks["F"], pbanks["G"], pbanks["H"]

    xnS = sbuf_in(top, "xnS", (128, NDT * T), BF16)
    identS = sbuf_in(top, "identS", (128, 128), F32)
    c0S = sbuf_in(top, "c0S", (128, 512), F32)
    onecS = sbuf_in(top, "onecS", (128, 1), BF16)
    onecfS = sbuf_in(top, "onecfS", (128, 1), F32)
    onerS = sbuf_in(top, "onerS", (1, 128), BF16)
    onerfS = sbuf_in(top, "onerfS", (1, 128), F32)
    meancS = sbuf_in(top, "meancS", (128, 1), F32)
    blkhS = sbuf_in(top, "blkhS", (128, 8 * 16), BF16)
    ldbS = sbuf_in(top, "ldbS", (128, 16), F32)
    lddS = sbuf_in(top, "lddS", (128, 16 * 19), F32)
    bsafeS = sbuf_in(top, "bsafeS", (16, 1), F32)
    bsgS = sbuf_in(top, "bsgS", (128, 4), F32)
    knmS = sbuf_in(top, "knmS", (16, 1), F32)
    bqS = sbuf_in(top, "bqS", (128, 3 * NDT), F32)
    b1S = sbuf_in(top, "b1S", (128, FF // 128), F32)
    b2S = sbuf_in(top, "b2S", (128, NDT), F32)
    binS = sbuf_in(top, "binS", (128, 2 * DI // 128), F32)
    dvecS = sbuf_in(top, "dvecS", (128, DI // 128), F32)
    bout1S = sbuf_in(top, "bout1S", (128, NDT), F32)
    bout0S = sbuf_in(top, "bout0S", (128, NDT), F32)
    cbank = sbuf_in(top, "cbank", (128, 4), F32)

    def dma(q, outap, inap, sem="sd", res=None, waits=()):
        w = list(waits) + (b.take(res) if res else [])
        b.emit(q, lambda e, o=outap, i=inap: e.dma_start(out=o, in_=i),
               waits=w, incs=[(sem, 16)])
        return b.val(sem)

    def mm(outap, lhsT, rhs, start, stop, waits=(), inc=True, tp=None,
           bank=None):
        w = list(waits)
        if bank is not None and start:
            w += b.take("bk" + bank)
        b.emit("tensor",
               lambda e, o=outap, l=lhsT, r=rhs, s1=start, s2=stop, t=tp:
               e.matmul(o, l, r, start=s1, stop=s2, tile_position=t),
               waits=w, incs=[])
        if inc and stop:
            b.emit("tensor", lambda e: e.drain(), incs=[("mm", 1)])
        return b.val("mm")

    def trans(outap, inap, waits=()):
        b.emit("tensor", lambda e, o=outap, i=inap:
               e.transpose(o, i, identS[:, :]), waits=waits, incs=[])
        b.emit("tensor", lambda e: e.drain(), incs=[("mm", 1)])
        return b.val("mm")

    def ev(fn, waits=(), bank=None, dr=False):
        if dr:
            b.emit("vector", fn, waits=waits, incs=[])
            b.emit("vector", lambda e: e.drain(), incs=[("vv", 1)])
        else:
            b.emit("vector", fn, waits=waits, incs=[("vv", 1)])
        t = b.val("vv")
        if bank is not None:
            b.rel("bk" + bank, [("vv", t)])
        return t

    def sc(fn, waits=(), bank=None, dr=False):
        if dr:
            b.emit("scalar", fn, waits=waits, incs=[])
            b.emit("scalar", lambda e: e.drain(), incs=[("ss", 1)])
        else:
            b.emit("scalar", fn, waits=waits, incs=[("ss", 1)])
        t = b.val("ss")
        if bank is not None:
            b.rel("bk" + bank, [("ss", t)])
        return t

    ev(lambda e: e.memset(cbank[:, 0:1], 1e-5))
    ev(lambda e: e.memset(cbank[:, 1:2], 1e-30))
    ev(lambda e: e.memset(cbank[:, 2:3], float(np.log(1.02))))
    for dst, src in ((identS, ident_h), (c0S, c0_h), (onecS, ones_col_h),
                     (onecfS, ones_col_f_h),
                     (onerS, ones_row_h), (onerfS, ones_row_f_h),
                     (meancS, mean_col_h), (blkhS, blkh_h)):
        dma("sync", dst[:, :], src[:, :])
    dma("sync", bqS[:, :], bqkv[:, :])
    dma("sync", b1S[:, :], b1_d[:, :])
    dma("sync", b2S[:, :], b2_d[:, :])
    dma("sync", binS[:, :], bin_d[:, :])
    dma("sync", dvecS[:, :], dvec_d[:, :])
    dma("sync", bout1S[:, :], bout1_d[:, :])
    t_cl = dma("sync", bout0S[:, :], bout0_d[:, :])
    su = contextlib.ExitStack()
    ldr = sbuf_in(su, "ldr", (1, 4 * 16), F32)
    bs16 = sbuf_in(su, "bs16", (16, 1), F32)
    sprhS = sbuf_in(su, "sprhS", (16, 128), F32)
    selp4S = sbuf_in(su, "selp4S", (16, 4), F32)
    bsrhs = sbuf_in(su, "bsrhs", (16, 4), F32)
    dma("sync", ldr[0:1, 0:16], dlog_d[:, :].transpose([1, 0]))
    dma("sync", bs16[:, :], bscale_d[:, :])
    dma("sync", sprhS[:, :], spreadh_h[:, :])
    t_cl = dma("sync", selp4S[:, :], selp4_h[:, :])
    t = sc(lambda e: e.activation(ldr[0:1, 16:32], ldr[0:1, 0:16], AF.Exp,
                                  scale=-1.0), waits=[("sd", t_cl)])
    t = ev(lambda e: e.tensor_scalar_add(ldr[0:1, 32:48], ldr[0:1, 16:32], 1.0),
           waits=[("ss", t)])
    t = sc(lambda e: e.activation(ldr[0:1, 48:64], ldr[0:1, 32:48], AF.Ln),
           waits=[("vv", t)])
    t_ld = ev(lambda e: e.tensor_scalar_mul(ldr[0:1, 0:16], ldr[0:1, 48:64],
                                            -1.0), waits=[("ss", t)])
    ev(lambda e: e.tensor_scalar(bsafeS[:, :], bs16[:, :], 1.05, 1e-6,
                                 ALU.mult, ALU.add), waits=[("sd", t_cl)])
    t = mm(pC[0:128, 0:16], onerfS[:, :], ldr[0:1, 0:16], True, True,
           waits=[("vv", t_ld)])
    t_ldb = ev(lambda e: e.tensor_copy(ldbS[:, :], pC[0:128, 0:16]),
               waits=[("mm", t)], bank="C", dr=True)
    for di in range(19):
        ev(lambda e, di=di: e.tensor_scalar_mul(
            lddS[:, :].rearrange("p (h t) -> p h t", t=19)[:, :, di],
            ldbS[:, :], float(128 * (di - 3))),
           waits=[("vv", t_ldb)] if di == 0 else [])
    t = ev(lambda e: e.tensor_scalar(bsrhs[:, :], selp4S[:, :], bs16[:, :],
                                     None, ALU.mult), waits=[("sd", t_cl)])
    t = mm(pD[0:128, 0:4], sprhS[:, :], bsrhs[:, :], True, True,
           waits=[("vv", t)])
    ev(lambda e: e.tensor_copy(bsgS[:, :], pD[0:128, 0:4]), waits=[("mm", t)],
       bank="D")
    ev(lambda e: e.tensor_scalar_mul(bqS[:, 0:NDT], bqS[:, 0:NDT], 0.125))
    b.barrier()
    dump("ldb", ldbS[:, :])
    dump("ldd", lddS[:, :])
    dump("bsg", bsgS[:, :])
    dump("bsafe", bsafeS[:, :])
    dump("ldr", ldr[0:1, :])
    su.close()

    def stage_ln(x_src, tag):
        ar = contextlib.ExitStack()
        xb = sbuf_in(ar, "ln_xb" + tag, (128, NDT * 512), F32)
        sqb = sbuf_in(ar, "ln_sqb" + tag, (128, NDT * 512), BF16)
        r1 = sbuf_in(ar, "ln_r1" + tag, (1, 512 * 4), F32)
        for c in range(NJC):
            tok_ld = 0
            for dt_ in range(NDT):
                tok_ld = dma("sync", xb[:, dt_ * 512:(dt_ + 1) * 512],
                             x_src[dt_ * 128:(dt_ + 1) * 128,
                                   c * 512:(c + 1) * 512], res="ln_xb")
            for dt_ in range(NDT):
                t_mean = mm(pA[0:1, :], meancS[:, :],
                            xb[:, dt_ * 512:(dt_ + 1) * 512],
                            dt_ == 0, dt_ == NDT - 1,
                            waits=[("sd", tok_ld)] if dt_ == 0 else [],
                            bank="A")
            t_mer = ev(lambda e: e.tensor_copy(r1[0:1, 0:512], pA[0:1, :]),
                       waits=[("mm", t_mean)], bank="A", dr=True)
            for dt_ in range(NDT):
                t_sq = sc(lambda e, dt_=dt_: e.activation(
                    sqb[:, dt_ * 512:(dt_ + 1) * 512],
                    xb[:, dt_ * 512:(dt_ + 1) * 512], AF.Square),
                    waits=[("sd", tok_ld)] if dt_ == 0 else [])
            for dt_ in range(NDT):
                t_msq = mm(pB[0:1, :], onecS[:, :],
                           sqb[:, dt_ * 512:(dt_ + 1) * 512],
                           dt_ == 0, dt_ == NDT - 1,
                           waits=[("ss", t_sq)] if dt_ == 0 else [],
                           bank="B")
            t1 = ev(lambda e: e.tensor_scalar_mul(r1[0:1, 512:1024], pB[0:1, :],
                                                  1.0 / D),
                    waits=[("mm", t_msq)], bank="B", dr=True)
            ev(lambda e: e.tensor_tensor(r1[0:1, 1024:1536], r1[0:1, 0:512],
                                         r1[0:1, 0:512], ALU.mult),
               waits=[("vv", t_mer)], dr=True)
            t3 = ev(lambda e: e.tensor_tensor(r1[0:1, 1024:1536],
                                              r1[0:1, 512:1024],
                                              r1[0:1, 1024:1536], ALU.subtract),
                    dr=True)
            t4 = sc(lambda e: e.activation(r1[0:1, 1536:2048],
                                           r1[0:1, 1024:1536], AF.Ln,
                                           bias=cbank[0:1, 0:1]),
                    waits=[("vv", t3)], dr=True)
            t5 = sc(lambda e: e.activation(r1[0:1, 1024:1536],
                                           r1[0:1, 1536:2048], AF.Exp,
                                           scale=-0.5))
            tm = mm(pC[:, :], onerfS[:, :], r1[0:1, 0:512], True, True,
                    waits=[("ss", t5)], bank="C")
            tr = mm(pD[:, :], onerfS[:, :], r1[0:1, 1024:1536], True, True,
                    bank="D")
            t_last = 0
            for dt_ in range(NDT):
                ev(lambda e, dt_=dt_: e.tensor_tensor(
                    xb[:, dt_ * 512:(dt_ + 1) * 512],
                    xb[:, dt_ * 512:(dt_ + 1) * 512], pC[:, :], ALU.subtract),
                   waits=[("mm", tr)] if dt_ == 0 else [], dr=True)
                t_last = ev(lambda e, dt_=dt_, c=c: e.tensor_tensor(
                    xnS[:, dt_ * T + c * 512: dt_ * T + (c + 1) * 512],
                    xb[:, dt_ * 512:(dt_ + 1) * 512], pD[:, :], ALU.mult))
            b.rel("bkC", [("vv", t_last)])
            b.rel("bkD", [("vv", t_last)])
            b.rel("ln_xb", [("vv", t_last)])
        ar.close()

    stage_ln(x_in, "0")
    b.barrier()
    if probe:
        dma("sync", xn_scr[:, :], xnS[:, :], "pr")
        b.emit("sync", lambda e: e.nop(), waits=[("pr", b.val("pr"))])
    dump("xn0", xnS[:, :])

    l0ar = contextlib.ExitStack()
    jwS = sbuf_in(l0ar, "jwS", (128, 4 * T), BF16)
    rlS = sbuf_in(l0ar, "rlS", (128, 4 * T), BF16)
    la = contextlib.ExitStack()
    wlS = sbuf_in(la, "wlS", (128, NDT * 256), BF16)
    wprojS = sbuf_in(la, "wprojS", (64, 4 * T), BF16)
    ppiS = sbuf_in(la, "ppiS", (64, 4 * 128), BF16)
    ppjS = sbuf_in(la, "ppjS", (64, 4 * 128), BF16)
    sum6S = sbuf_in(la, "sum6S", (128, 4), BF16)
    spread6S = sbuf_in(la, "spread6S", (4, 128), BF16)
    j6t4S = sbuf_in(la, "j6t4S", (128, 128), BF16)
    e13 = sbuf_in(la, "e13", (128, 2 * 512), F32)
    lbuf = sbuf_in(la, "lbuf", (128, 512), F32)
    lsq = sbuf_in(la, "lsq", (128, 512), BF16)
    rinv = sbuf_in(la, "rinv", (4, 512), BF16)
    rln = sbuf_in(la, "rln", (4, 512), F32)
    lnorm = sbuf_in(la, "lnorm", (128, 512), BF16)
    dma("sync", ppiS[:, :], ppi_h[:, :])
    dma("sync", ppjS[:, :], ppj_h[:, :])
    dma("sync", sum6S[:, :], sum6_h[:, :])
    dma("sync", spread6S[:, :], spread6_h[:, :])
    t_lc = dma("sync", j6t4S[:, :], j6t4_h[:, :])
    t_wl = dma("gpsimd", wlS[:, :].rearrange("p (c f) -> p c f", c=NDT),
               wlinesT[:, :].rearrange("(c p) f -> p c f", p=128), "gd")
    for mat in range(4):
        for c in range(NJC):
            for dt_ in range(NDT):
                lhsT = wlS[:, dt_ * 256 + mat * 64: dt_ * 256 + mat * 64 + 64]
                if mat == 0:
                    if c == 0:
                        rhs = xnS[:, dt_ * T + 0: dt_ * T + 511]
                        o = pA[0:64, 1:512]
                    else:
                        rhs = xnS[:, dt_ * T + c * 512 - 1:
                                  dt_ * T + (c + 1) * 512 - 1]
                        o = pA[0:64, 0:512]
                else:
                    rhs = xnS[:, dt_ * T + c * 512: dt_ * T + (c + 1) * 512]
                    o = pA[0:64, 0:512]
                t_mmp = mm(o, lhsT, rhs, dt_ == 0, dt_ == NDT - 1,
                           waits=[("gd", t_wl)] if dt_ == 0 else [], bank="A")
            if mat == 0 and c == 0:
                t_evp = ev(lambda e: e.tensor_copy(
                    wprojS[0:64, 1:512], pA[0:64, 1:512]),
                    waits=[("mm", t_mmp)], bank="A")
                ev(lambda e: e.memset(wprojS[0:64, 0:1], 0.0))
            else:
                t_evp = ev(lambda e, mat=mat, c=c: e.tensor_copy(
                    wprojS[0:64, mat * T + c * 512: mat * T + (c + 1) * 512],
                    pA[0:64, 0:512]), waits=[("mm", t_mmp)], bank="A")
    t_wproj = b.val("vv")
    for side, (m1, m2, dstS) in enumerate(((0, 1, jwS), (2, 3, rlS))):
        for g in range(4):
            for c in range(NJC):
                w1s = wprojS[0:64, m1 * T + c * 512: m1 * T + (c + 1) * 512]
                w2s = wprojS[0:64, m2 * T + c * 512: m2 * T + (c + 1) * 512]
                pgi = ppiS[:, g * 128:(g + 1) * 128]
                pgj = ppjS[:, g * 128:(g + 1) * 128]
                wv = [("vv", t_wproj)]
                t1 = mm(pA[:, :], pgi, w1s, True, True, waits=wv, bank="A")
                t2 = mm(pB[:, :], pgj, w2s, True, True, bank="B")
                t3 = mm(pC[:, :], pgj, w1s, True, True, bank="C")
                t4 = mm(pD[:, :], pgi, w2s, True, True, bank="D")
                s1 = sc(lambda e: e.activation(e13[:, 0:512], pA[:, :],
                                               AF.Identity),
                        waits=[("mm", t1)] + b.take("e13"), bank="A")
                s2 = sc(lambda e: e.activation(e13[:, 512:1024], pC[:, :],
                                               AF.Identity),
                        waits=[("mm", t3)], bank="C")
                v1 = ev(lambda e: e.tensor_tensor(lbuf[:, :], e13[:, 0:512],
                                                  pB[:, :], ALU.mult),
                        waits=[("ss", s1), ("mm", t2)] + b.take("lbuf"),
                        bank="B", dr=True)
                v2 = ev(lambda e: e.scalar_tensor_tensor(
                    e13[:, 512:1024], e13[:, 512:1024], -1.0, pD[:, :],
                    ALU.mult, ALU.mult), waits=[("ss", s2), ("mm", t4)],
                    bank="D", dr=True)
                v3 = ev(lambda e: e.tensor_tensor(lbuf[:, :], lbuf[:, :],
                                                  e13[:, 512:1024], ALU.add),
                        dr=True)
                b.rel("e13", [("vv", v3)])
                if side == 0 and c == 0:
                    v3 = ev(lambda e: e.memset(lbuf[:, 0:1], 0.0), dr=True)
                v4 = ev(lambda e: e.tensor_tensor(lsq[:, :], lbuf[:, :],
                                                  lbuf[:, :], ALU.mult),
                        waits=b.take("lsq"))
                t5 = mm(pE[0:4, :], sum6S[:, :], lsq[:, :], True, True,
                        waits=[("vv", v4)], bank="E")
                b.rel("lsq", [("mm", t5)])
                v5 = ev(lambda e: e.tensor_scalar_max(rln[:, :], pE[0:4, :],
                                                      1e-24),
                        waits=[("mm", t5)] + b.take("rln"), bank="E")
                s3 = sc(lambda e: e.activation(rln[:, :], rln[:, :], AF.Ln),
                        waits=[("vv", v5)], dr=True)
                s4 = sc(lambda e: e.activation(rinv[:, :], rln[:, :], AF.Exp,
                                               scale=-0.5),
                        waits=b.take("rinv"))
                b.rel("rln", [("ss", s4)])
                t6 = mm(pF[:, :], spread6S[:, :], rinv[:, :], True, True,
                        waits=[("ss", s4)], bank="F")
                b.rel("rinv", [("mm", t6)])
                if side == 1:
                    vfin = ev(lambda e, g=g, c=c: e.tensor_tensor(
                        rlS[:, g * T + c * 512: g * T + (c + 1) * 512],
                        lbuf[:, :], pF[:, :], ALU.mult),
                        waits=[("mm", t6)], bank="F", dr=True)
                    b.rel("lbuf", [("vv", vfin)])
                else:
                    v6 = ev(lambda e: e.tensor_tensor(lnorm[:, :], lbuf[:, :],
                                                      pF[:, :], ALU.mult),
                            waits=[("mm", t6)] + b.take("lnorm"), bank="F")
                    b.rel("lbuf", [("vv", v6)])
                    t7 = mm(pG[:, :], j6t4S[:, :], lnorm[:, :], True, True,
                            waits=[("vv", v6)], bank="G")
                    b.rel("lnorm", [("mm", t7)])
                    vfin = ev(lambda e, g=g, c=c: e.tensor_scalar(
                        jwS[:, g * T + c * 512: g * T + (c + 1) * 512],
                        pG[:, :], bsgS[:, g:g + 1], None, ALU.mult),
                        waits=[("mm", t7)], bank="G", dr=True)
    dump("wproj", wprojS[0:64, :])
    b.barrier()
    dump("jw", jwS[:, :])
    dump("rl", rlS[:, :])
    la.close()

    kS = sbuf_in(l0ar, "kS", (128, 8 * T), BF16)
    vS = sbuf_in(l0ar, "vS", (128, NTT * 1024), BF16)
    qa = contextlib.ExitStack()
    wkvA = sbuf_in(qa, "wkvA", (128, NDT * 128), BF16)
    wkvB = sbuf_in(qa, "wkvB", (128, NDT * 128), BF16)
    wvS = sbuf_in(qa, "wvS", (128, NDT * 1024), BF16)
    ksq = sbuf_in(qa, "ksq", (128, 512), BF16)
    kn2 = sbuf_in(qa, "kn2", (16, T), F32)
    knr = sbuf_in(qa, "knr", (16, 4), F32)
    bvb = sbuf_in(qa, "bvb", (128, 1024), F32)
    bvrow = sbuf_in(qa, "bvrowS", (1, 1024), F32)
    t_wv = dma("gpsimd", wvS[:, :].rearrange("p (c f) -> p c f", c=NDT),
               wqkvT[:, 2 * D:3 * D].rearrange("(c p) f -> p c f", p=128),
               "gd")
    t_bv = dma("sync", bvrow[0:1, :], bvrow_d[0:1, :])
    t_bvb = 0
    for jj in range(2):
        t_b = mm(pH[:, 0:512], onerfS[:, :],
                 bvrow[0:1, jj * 512:(jj + 1) * 512], True, True,
                 waits=[("sd", t_bv)], bank="H")
        t_bvb = ev(lambda e, jj=jj: e.tensor_copy(
            bvb[:, jj * 512:(jj + 1) * 512], pH[:, 0:512]),
            waits=[("mm", t_b)], bank="H")
    for j in range(8):
        wbuf = wkvA if j % 2 == 0 else wkvB
        t_wk = dma("gpsimd", wbuf[:, :].rearrange("p (c f) -> p c f", c=NDT),
                   wqkvT[:, D + j * 128: D + (j + 1) * 128].rearrange(
                       "(c p) f -> p c f", p=128), "gd",
                   res="wk%d" % (j % 2))
        for c in range(NJC):
            bk = "A" if c % 2 == 0 else "B"
            po = pbanks[bk]
            for dt_ in range(NDT):
                t_mmk = mm(po[:, :], wbuf[:, dt_ * 128:(dt_ + 1) * 128],
                           xnS[:, dt_ * T + c * 512: dt_ * T + (c + 1) * 512],
                           dt_ == 0, dt_ == NDT - 1,
                           waits=[("gd", t_wk)] if dt_ == 0 else [], bank=bk)
            s_k = sc(lambda e, j=j, c=c, po=po: e.activation(
                kS[:, j * T + c * 512: j * T + (c + 1) * 512], po[:, :],
                AF.Identity, bias=bqS[:, NDT + j: NDT + j + 1]),
                waits=[("mm", t_mmk)], bank=bk, dr=True)
            if c == NJC - 1:
                b.rel("wk%d" % (j % 2), [("ss", s_k)])
            s_k2 = sc(lambda e, j=j, c=c: e.activation(
                ksq[:, :], kS[:, j * T + c * 512: j * T + (c + 1) * 512],
                AF.Square), waits=b.take("ksq"))
            t_kn = mm(pE[0:16, :], blkhS[:, j * 16:(j + 1) * 16], ksq[:, :],
                      True, True, waits=[("ss", s_k2)], bank="E")
            b.rel("ksq", [("mm", t_kn)])
            if j == 0:
                ev(lambda e, c=c: e.tensor_copy(kn2[:, c * 512:(c + 1) * 512],
                                                pE[0:16, :]),
                   waits=[("mm", t_kn)], bank="E", dr=True)
            else:
                ev(lambda e, c=c: e.tensor_tensor(
                    kn2[:, c * 512:(c + 1) * 512],
                    kn2[:, c * 512:(c + 1) * 512], pE[0:16, :], ALU.add),
                   waits=[("mm", t_kn)], bank="E", dr=True)
    t_km = ev(lambda e: e.reduce_max(knr[:, 0:1], kn2[:, :], axis=AX.X))
    t_km2 = sc(lambda e: e.activation(knr[:, 1:2], knr[:, 0:1], AF.Ln,
                                      bias=cbank[0:16, 1:2]),
               waits=[("vv", t_km)], dr=True)
    t_knm = sc(lambda e: e.activation(knmS[:, :], knr[:, 1:2], AF.Exp,
                                      scale=0.5, bias=cbank[0:16, 2:3]))
    for tt in range(NTT):
        for jj in range(2):
            bk = "C" if jj == 0 else "D"
            po = pbanks[bk]
            for dt_ in range(NDT):
                t_mmv = mm(po[:, :],
                           xnS[:, dt_ * T + tt * 128: dt_ * T + (tt + 1) * 128],
                           wvS[:, dt_ * 1024 + jj * 512:
                               dt_ * 1024 + (jj + 1) * 512],
                           dt_ == 0, dt_ == NDT - 1,
                           waits=([("gd", t_wv), ("vv", t_bvb)]
                                  if (tt == 0 and jj == 0 and dt_ == 0)
                                  else []), bank=bk)
            ev(lambda e, tt=tt, jj=jj, po=po: e.tensor_tensor(
                vS[:, tt * 1024 + jj * 512: tt * 1024 + (jj + 1) * 512],
                po[:, :], bvb[:, jj * 512:(jj + 1) * 512], ALU.add),
               waits=[("mm", t_mmv)], bank=bk, dr=True)
    if probe:
        dma("sync", knm_scr[:, 0:T], kn2[:, :], "pr",
            waits=[("ss", t_knm), ("vv", b.val("vv"))])
        dma("sync", knm2_scr[:, 0:1], knmS[:, :], "pr")
        dma("sync", knr_scr[:, 0:1], knr[:, 0:1], "pr")
        b.emit("sync", lambda e: e.nop(), waits=[("pr", b.val("pr"))])
    dump("k", kS[:, :])
    dump("v", vS[:, :])
    dump("kn2", kn2[:, :])
    dump("knm", knmS[:, :])
    b.barrier()
    qa.close()

    aa = contextlib.ExitStack()
    wqA = sbuf_in(aa, "wqA", (128, NDT * 128), BF16)
    wqB = sbuf_in(aa, "wqB", (128, NDT * 128), BF16)
    qch = sbuf_in(aa, "qch", (128, 8 * 512), BF16)
    qsq = sbuf_in(aa, "qsq", (128, 512), BF16)
    selhS = sbuf_in(aa, "selhS", (16, 16 * 128), BF16)
    qn = sbuf_in(aa, "qn", (16, 2 * 512), F32)
    bnd = sbuf_in(aa, "bnd", (16, 512), BF16)
    bbS = sbuf_in(aa, "bbS", (128, 512), F32)
    wexp = sbuf_in(aa, "wexp", (128, 512), F32)
    lsS = sbuf_in(aa, "lsS", (128, 2 * 512), F32)
    ptS = sbuf_in(aa, "ptS", (128, 2 * 512), BF16)
    posm = sbuf_in(aa, "posm", (128, 4 * 512), BF16)
    posge = sbuf_in(aa, "posge", (128, 4 * 512), BF16)
    c0mS = sbuf_in(aa, "c0mS", (128, 4 * 512), BF16)
    sattS = sbuf_in(aa, "sattS", (128, 8 * 512), BF16)
    rbS = sbuf_in(aa, "rbS", (128, 512), BF16)
    denS = sbuf_in(aa, "denS", (1, 512), F32)
    denB = sbuf_in(aa, "denB", (1, 512), BF16)
    woS = sbuf_in(aa, "woS", (128, NDT * 1024), BF16)
    soutS = sbuf_in(aa, "soutS", (128, 512), F32)

    dma("sync", selhS[:, :], selh_h[:, :])
    t_wo = dma("gpsimd", woS[:, :].rearrange("p (c f) -> p c f", c=NDT),
               woutT_d[:, :].rearrange("(c p) f -> p c f", p=128), "gd")
    for bi in range(4):
        ev(lambda e, bi=bi: e.tensor_scalar(
            posm[:, bi * 512:(bi + 1) * 512], c0S[:, :], float(128 * bi),
            None, ALU.is_gt))
        ev(lambda e, bi=bi: e.tensor_scalar(
            posge[:, bi * 512:(bi + 1) * 512], c0S[:, :], float(128 * bi),
            None, ALU.is_ge))
        ev(lambda e, bi=bi: e.tensor_scalar_max(
            c0mS[:, bi * 512:(bi + 1) * 512], c0S[:, :], float(128 * bi)))
    t_masks = b.val("vv")

    for qc in range(NJC):
        ktmax = 4 * (qc + 1)
        for j in range(8):
            wbuf = wqA if j % 2 == 0 else wqB
            t_wqj = dma("gpsimd",
                        wbuf[:, :].rearrange("p (c f) -> p c f", c=NDT),
                        wqkvT[:, j * 128:(j + 1) * 128].rearrange(
                            "(c p) f -> p c f", p=128), "gd",
                        res="wq%d" % (j % 2))
            bk = "A" if j % 2 == 0 else "B"
            po = pbanks[bk]
            for dt_ in range(NDT):
                t_mmq = mm(po[:, :], wbuf[:, dt_ * 128:(dt_ + 1) * 128],
                           xnS[:, dt_ * T + qc * 512: dt_ * T + (qc + 1) * 512],
                           dt_ == 0, dt_ == NDT - 1,
                           waits=[("gd", t_wqj)] if dt_ == 0 else [],
                           bank=bk)
            b.rel("wq%d" % (j % 2), [("mm", t_mmq)])
            s_q = sc(lambda e, j=j, po=po: e.activation(
                qch[:, j * 512:(j + 1) * 512], po[:, :], AF.Identity,
                scale=0.125, bias=bqS[:, j:j + 1]),
                waits=[("mm", t_mmq)] + (b.take("qch") if j == 0 else []),
                bank=bk, dr=True)
            s_q2 = sc(lambda e, j=j: e.activation(
                qsq[:, :], qch[:, j * 512:(j + 1) * 512], AF.Square),
                waits=b.take("qsq"))
            t_qn = mm(pE[0:16, :], blkhS[:, j * 16:(j + 1) * 16], qsq[:, :],
                      True, True, waits=[("ss", s_q2)], bank="E")
            b.rel("qsq", [("mm", t_qn)])
            if j == 0:
                ev(lambda e: e.tensor_copy(qn[:, 0:512], pE[0:16, :]),
                   waits=[("mm", t_qn)] + b.take("qn"), bank="E", dr=True)
            else:
                ev(lambda e: e.tensor_tensor(qn[:, 0:512], qn[:, 0:512],
                                             pE[0:16, :], ALU.add),
                   waits=[("mm", t_qn)], bank="E", dr=True)
        t = sc(lambda e: e.activation(qn[:, 512:1024], qn[:, 0:512], AF.Ln,
                                      bias=cbank[0:16, 1:2]),
               waits=[("vv", b.val("vv")), ("ss", t_knm)], dr=True)
        t = sc(lambda e: e.activation(qn[:, 0:512], qn[:, 512:1024], AF.Exp,
                                      scale=0.5))
        t_bnd = ev(lambda e: e.tensor_scalar(bnd[:, :], qn[:, 0:512],
                                             knmS[:, :], bsafeS[:, :],
                                             ALU.mult, ALU.add),
                   waits=[("ss", t)] + b.take("bnd"))
        b.rel("qn", [("vv", t_bnd)])
        if qc == 0:
            b.barrier()
            dump("qch0", qch[:, :])
            dump("bnd0", bnd[:, :])
            b.barrier()

        for h in range(16):
            t_bb = mm(pE[:, :], selhS[:, h * 128:(h + 1) * 128], bnd[:, :],
                      True, True, waits=[("vv", t_bnd)], bank="E")
            t_bbe = ev(lambda e: e.tensor_copy(bbS[:, :], pE[:, :]),
                       waits=[("mm", t_bb)] + b.take("bbS"), bank="E",
                       dr=True)
            g4, hh4 = h // 4, h % 4
            j2, p2 = h // 2, h % 2
            av_bk = "G" if p2 == 0 else "H"
            pav = pbanks[av_bk]
            avsl = pav[64 * p2:64 * p2 + 64, 0:512]
            t_av = t_dn = 0
            for kt in range(ktmax):
                par = kt % 2
                qk_bk = "A" if par == 0 else "C"
                in_bk = "B" if par == 0 else "D"
                pqk = pbanks[qk_bk]
                pin = pbanks[in_bk]
                di = qc * 4 - kt + 3
                diag = kt >= ktmax - 4
                bi = kt - qc * 4
                wsrc = c0mS[:, bi * 512:(bi + 1) * 512] if diag else c0S[:, :]
                s_w = sc(lambda e, h=h, di=di, wsrc=wsrc:
                         e.activation(
                             wexp[:, 0:512], wsrc, AF.Exp,
                             scale=ldbS[:, h:h + 1],
                             bias=lddS[:, h * 19 + di: h * 19 + di + 1]),
                         waits=b.take("wexp") +
                         ([("vv", t_masks)] if diag else []))
                t_qk = mm(pqk[:, :],
                          kS[64 * p2:64 * p2 + 64,
                             j2 * T + kt * 128: j2 * T + (kt + 1) * 128],
                          qch[64 * p2:64 * p2 + 64,
                              j2 * 512:(j2 + 1) * 512],
                          True, True, bank=qk_bk)
                t_in = mm(pin[:, :],
                          jwS[32 * hh4:32 * hh4 + 6,
                              g4 * T + kt * 128: g4 * T + (kt + 1) * 128],
                          rlS[32 * hh4:32 * hh4 + 6,
                              g4 * T + qc * 512: g4 * T + (qc + 1) * 512],
                          True, True, bank=in_bk,
                          tp=(32 * hh4, 0) if hh4 else None)
                if debug and qc == 0 and h == 0:
                    spare = lsS[:, (1 - par) * 512:(2 - par) * 512]
                    tdbg = ev(lambda e, spare=spare, pqk=pqk: e.tensor_copy(
                        spare, pqk[:, :]), waits=[("mm", t_qk)], dr=True)
                    dump("qkps_kt%d" % kt, spare)
                    tdbg = ev(lambda e, spare=spare, pin=pin: e.tensor_copy(
                        spare, pin[:, :]), waits=[("mm", t_in)], dr=True)
                    dump("inps_kt%d" % kt, spare)
                if diag:
                    ev(lambda e, bi=bi: e.tensor_tensor(
                        wexp[:, 0:512], wexp[:, 0:512],
                        posm[:, bi * 512:(bi + 1) * 512], ALU.mult),
                       waits=[("ss", s_w), ("vv", t_masks)], dr=True)
                v1 = ev(lambda e, par=par, pin=pin: e.tensor_tensor(
                    lsS[:, par * 512:(par + 1) * 512],
                    wexp[:, 0:512], pin[:, :], ALU.mult),
                    waits=[("mm", t_in), ("ss", s_w)] + b.take("ls%d" % par),
                    bank=in_bk, dr=True)
                b.rel("wexp", [("vv", v1)])
                v2 = ev(lambda e, par=par, pqk=pqk: e.scalar_tensor_tensor(
                    lsS[:, par * 512:(par + 1) * 512], pqk[:, :], 1.0,
                    lsS[:, par * 512:(par + 1) * 512], ALU.mult, ALU.add),
                    waits=[("mm", t_qk), ("vv", t_bbe)], bank=qk_bk, dr=True)
                v3 = ev(lambda e, par=par: e.tensor_tensor(
                    lsS[:, par * 512:(par + 1) * 512],
                    lsS[:, par * 512:(par + 1) * 512], bbS[:, :],
                    ALU.subtract))
                if debug and qc == 0 and h == 0:
                    dump("ls_kt%d" % kt, lsS[:, par * 512:(par + 1) * 512])
                    dump("wx_kt%d" % kt, wexp[:, 0:512])
                    dump("bb_kt%d" % kt, bbS[:, :])
                s_p = sc(lambda e, par=par: e.activation(
                    ptS[:, par * 512:(par + 1) * 512],
                    lsS[:, par * 512:(par + 1) * 512], AF.Exp),
                    waits=[("vv", v3)] + b.take("pt%d" % par))
                b.rel("ls%d" % par, [("ss", s_p)])
                t_pm = ("ss", s_p)
                if diag:
                    v_pm = ev(lambda e, par=par, bi=bi: e.tensor_tensor(
                        ptS[:, par * 512:(par + 1) * 512],
                        ptS[:, par * 512:(par + 1) * 512],
                        posge[:, bi * 512:(bi + 1) * 512], ALU.mult),
                        waits=[("ss", s_p)])
                    t_pm = ("vv", v_pm)
                t_dn = mm(pF[0:1, :], onecS[:, :],
                          ptS[:, par * 512:(par + 1) * 512],
                          kt == 0, kt == ktmax - 1,
                          waits=[t_pm], bank="F")
                t_av = mm(avsl,
                          vS[:, kt * 1024 + h * 64: kt * 1024 + h * 64 + 64],
                          ptS[:, par * 512:(par + 1) * 512],
                          kt == 0, kt == ktmax - 1, bank=av_bk,
                          tp=(0, 64) if p2 else None)
                b.rel("pt%d" % par, [("mm", t_av)])
            t_de = ev(lambda e: e.tensor_scalar_add(denS[0:1, 0:512],
                                                    pF[0:1, :], 1e-30),
                      waits=[("mm", t_dn)], bank="F", dr=True)
            ev(lambda e: e.reciprocal(qn[0:1, 512:1024], denS[0:1, 0:512]),
               dr=True)
            t_rcb = ev(lambda e: e.tensor_copy(denB[0:1, :],
                                               qn[0:1, 512:1024]), dr=True)
            t_rb = mm(pE[:, :], onerS[:, :], denB[0:1, :], True, True,
                      waits=[("vv", t_rcb)], bank="E")
            t_rbe = ev(lambda e: e.tensor_copy(rbS[:, :], pE[:, :]),
                       waits=[("mm", t_rb)] + b.take("rbS"), bank="E",
                       dr=True)
            vout = ev(lambda e, j2=j2, p2=p2, pav=pav: e.tensor_tensor(
                sattS[64 * p2:64 * p2 + 64, j2 * 512:(j2 + 1) * 512],
                pav[64 * p2:64 * p2 + 64, 0:512], rbS[64 * p2:64 * p2 + 64, :],
                ALU.mult),
                waits=[("vv", t_rbe), ("mm", t_av)] +
                (b.take("sattS") if h == 0 else []))
            b.rel("bk" + av_bk, [("vv", vout)])
            b.rel("rbS", [("vv", vout)])
            b.rel("bbS", [("vv", v3)])
        t_sat = b.val("vv")
        if probe:
            dma("sync", satt_scr[:, qc * 4096:(qc + 1) * 4096], sattS[:, :],
                "pr", waits=[("vv", t_sat)])
            dma("sync", bnd_scr[:, qc * 512:(qc + 1) * 512], bnd[:, :],
                "pr", waits=[("vv", t_sat)])
            b.emit("sync", lambda e: e.nop(), waits=[("pr", b.val("pr"))])
        if qc == 0:
            b.barrier()
            dump("satt0", sattS[:, :])
            b.barrier()
            t_sat = b.val("vv")
        for dt_ in range(NDT):
            bk = "A" if dt_ % 2 == 0 else "B"
            po = pbanks[bk]
            for j in range(8):
                t_mo = mm(po[:, :],
                          woS[:, j * 1024 + dt_ * 128: j * 1024 + (dt_ + 1) * 128],
                          sattS[:, j * 512:(j + 1) * 512], j == 0, j == 7,
                          waits=[("gd", t_wo), ("vv", t_sat)] if j == 0 else [],
                          bank=bk)
            s_o = sc(lambda e, dt_=dt_, po=po: e.activation(
                soutS[:, 0:512], po[:, :],
                AF.Identity, bias=bout0S[:, dt_:dt_ + 1]),
                waits=[("mm", t_mo)] + b.take("so0"), bank=bk)
            t_so = dma("sync", s_scr[dt_ * 128:(dt_ + 1) * 128,
                                     qc * 512:(qc + 1) * 512],
                       soutS[:, 0:512], "so", waits=[("ss", s_o)])
            if probe:
                t_so = dma("sync", s0_scr[dt_ * 128:(dt_ + 1) * 128,
                                          qc * 512:(qc + 1) * 512],
                           soutS[:, 0:512], "so")
            b.rel("so0", [("so", t_so)])
        b.rel("qch", [("mm", b.val("mm"))])
        b.rel("sattS", [("mm", b.val("mm"))])
    b.barrier()
    aa.close()
    l0ar.close()

    fa = contextlib.ExitStack()
    w1tS = sbuf_in(fa, "w1tS", (128, NDT * FF), BF16)
    w2tS = sbuf_in(fa, "w2tS", (128, FF // 128 * D), BF16)
    stgA = sbuf_in(fa, "stgA", (128, 1024), F32)
    stgB = sbuf_in(fa, "stgB", (128, 1024), F32)
    terA = sbuf_in(fa, "terA", (128, 1024), BF16)
    terB = sbuf_in(fa, "terB", (128, 1024), BF16)
    accS = sbuf_in(fa, "accS", (128, 4), F32)
    thrS = sbuf_in(fa, "thrS", (128, 4), F32)
    totS = sbuf_in(fa, "totS", (1, 2), F32)
    hbuf = sbuf_in(fa, "hbuf", (128, 32 * 256), BF16)
    gbuf = sbuf_in(fa, "gbuf", (128, 2 * 256), F32)
    xfb = sbuf_in(fa, "xfb", (128, 2 * 256), F32)
    sfb = sbuf_in(fa, "sfb", (128, 2 * 256), F32)
    for wi, (wd, ntile) in enumerate(((w1T_d, D // 128), (w2T_d, FF // 128))):
        nch = (FF // 1024) if wi == 0 else 1
        width = 1024
        cnt = 0
        for ti in range(ntile):
            for chn in range(nch):
                stg = stgA if cnt % 2 == 0 else stgB
                t_l = dma("sync", stg[:, 0:width],
                          wd[ti * 128:(ti + 1) * 128,
                             chn * 1024: chn * 1024 + width],
                          res="stg%d" % (cnt % 2))
                t_r = ev(lambda e, stg=stg, width=width:
                         e.reduce_sum(accS[:, 2:3], stg[:, 0:width], axis=AX.X,
                                      apply_absolute_value=True),
                         waits=[("sd", t_l)], dr=True)
                if cnt == 0:
                    t_a = ev(lambda e, wi=wi: e.tensor_copy(
                        accS[:, wi:wi + 1], accS[:, 2:3]), dr=True)
                else:
                    t_a = ev(lambda e, wi=wi: e.tensor_tensor(
                        accS[:, wi:wi + 1], accS[:, wi:wi + 1], accS[:, 2:3],
                        ALU.add), dr=True)
                b.rel("stg%d" % (cnt % 2), [("vv", t_r)])
                cnt += 1
    t_t1 = mm(pA[0:1, 0:2], onecfS[:, :], accS[:, 0:2],
              True, True, waits=[("vv", b.val("vv"))], bank="A")
    t_te = ev(lambda e: e.tensor_copy(totS[0:1, :], pA[0:1, 0:2]),
              waits=[("mm", t_t1)], bank="A")
    t_tb = mm(pB[0:128, 0:2], onerfS[:, :], totS[0:1, :], True, True,
              waits=[("vv", t_te)], bank="B")
    HALF1 = 0.5 / (FF * D)
    t_th = ev(lambda e: e.tensor_scalar_mul(thrS[:, 0:2], pB[0:128, 0:2],
                                            HALF1), waits=[("mm", t_tb)],
              bank="B", dr=True)
    t_th = ev(lambda e: e.tensor_scalar_mul(thrS[:, 2:4], thrS[:, 0:2], -1.0),
              dr=True)
    for wi, (wd, wt, ntile) in enumerate(((w1T_d, w1tS, D // 128),
                                          (w2T_d, w2tS, FF // 128))):
        nch = (FF // 1024) if wi == 0 else 1
        width = 1024
        cnt = 0
        for ti in range(ntile):
            for chn in range(nch):
                stg = stgA if cnt % 2 == 0 else stgB
                ter = terA if cnt % 2 == 0 else terB
                t_l = dma("sync", stg[:, 0:width],
                          wd[ti * 128:(ti + 1) * 128,
                             chn * 1024: chn * 1024 + width],
                          res="stg%d" % (cnt % 2))
                w0 = [("sd", t_l)] + ([("vv", t_th)] if cnt == 0 else [])
                va = ev(lambda e, stg=stg, ter=ter, width=width, wi=wi:
                        e.tensor_scalar(ter[:, 0:width], stg[:, 0:width],
                                        thrS[:, wi:wi + 1], None,
                                        ALU.is_gt),
                        waits=w0 + b.take("ter%d" % (cnt % 2)), dr=True)
                vb = ev(lambda e, stg=stg, width=width, wi=wi:
                        e.tensor_scalar(stg[:, 0:width], stg[:, 0:width],
                                        thrS[:, wi + 2: wi + 3], None,
                                        ALU.is_lt), dr=True)
                off = ti * (4096 if wi == 0 else 1024) + chn * 1024
                vc = ev(lambda e, stg=stg, ter=ter, width=width, wt=wt,
                        off=off: e.tensor_tensor(
                            wt[:, off: off + width], ter[:, 0:width],
                            stg[:, 0:width], ALU.subtract), dr=True)
                b.rel("stg%d" % (cnt % 2), [("vv", vc)])
                b.rel("ter%d" % (cnt % 2), [("vv", vc)])
                cnt += 1
    if debug:
        b.barrier()
        dump("w1t", w1tS[:, :])
        dump("thr", thrS[:, :])
        b.barrier()
    t_tern = b.val("vv")
    for cc in range(T // 256):
        c512 = cc // 2
        off256 = (cc % 2) * 256
        for fo in range(32):
            bk = "A" if fo % 2 == 0 else "B"
            po = pbanks[bk]
            for dt_ in range(NDT):
                t_up = mm(po[:, 0:256],
                          w1tS[:, dt_ * 4096 + fo * 128:
                               dt_ * 4096 + (fo + 1) * 128],
                          xnS[:, dt_ * T + cc * 256:(dt_) * T + (cc + 1) * 256],
                          dt_ == 0, dt_ == NDT - 1,
                          waits=[("vv", t_tern)] if (cc == 0 and fo == 0
                                                     and dt_ == 0) else [],
                          bank=bk)
            s_h = sc(lambda e, fo=fo, po=po: e.activation(
                hbuf[:, fo * 256:(fo + 1) * 256], po[:, 0:256], AF.Gelu,
                bias=b1S[:, fo:fo + 1]),
                waits=[("mm", t_up)] + (b.take("hbuf") if fo == 0 else []),
                bank=bk)
        t_hb = b.val("ss")
        if probe:
            dma("sync", h_scr[:, cc * 8192:(cc + 1) * 8192], hbuf[:, :],
                "pr", waits=[("ss", t_hb)])
            b.emit("scalar", lambda e: e.nop(), waits=[("pr", b.val("pr"))])
        for dt_ in range(NDT):
            bk = "C" if dt_ % 2 == 0 else "D"
            po = pbanks[bk]
            for fo in range(32):
                t_dw = mm(po[:, 0:256],
                          w2tS[:, fo * 1024 + dt_ * 128:
                               fo * 1024 + (dt_ + 1) * 128],
                          hbuf[:, fo * 256:(fo + 1) * 256],
                          fo == 0, fo == 31,
                          waits=[("ss", t_hb)] if fo == 0 else [], bank=bk)
            if dt_ == NDT - 1:
                b.rel("hbuf", [("mm", t_dw)])
            par = dt_ % 2
            s_g = sc(lambda e, dt_=dt_, po=po, par=par: e.activation(
                gbuf[:, par * 256:(par + 1) * 256], po[:, 0:256], AF.Silu,
                bias=b2S[:, dt_:dt_ + 1]),
                waits=[("mm", t_dw)] + b.take("gb%d" % par), bank=bk)
            t_lx = dma("sync", xfb[:, par * 256:(par + 1) * 256],
                       x_in[dt_ * 128:(dt_ + 1) * 128,
                            cc * 256:(cc + 1) * 256], res="xf%d" % par)
            t_ls = dma("sync", sfb[:, par * 256:(par + 1) * 256],
                       s_scr[dt_ * 128:(dt_ + 1) * 128,
                             cc * 256:(cc + 1) * 256], res="sf%d" % par,
                       waits=[("so", b.val("so"))])
            v_m = ev(lambda e, par=par: e.tensor_tensor(
                gbuf[:, par * 256:(par + 1) * 256],
                gbuf[:, par * 256:(par + 1) * 256],
                sfb[:, par * 256:(par + 1) * 256], ALU.mult),
                waits=[("ss", s_g), ("sd", t_ls)], dr=True)
            b.rel("sf%d" % par, [("vv", v_m)])
            v_x = ev(lambda e, par=par: e.tensor_tensor(
                xfb[:, par * 256:(par + 1) * 256],
                xfb[:, par * 256:(par + 1) * 256],
                gbuf[:, par * 256:(par + 1) * 256], ALU.add),
                waits=[("sd", t_lx)])
            if probe:
                dma("sync", g0_scr[dt_ * 128:(dt_ + 1) * 128,
                                   cc * 256:(cc + 1) * 256],
                    gbuf[:, par * 256:(par + 1) * 256], "pr",
                    waits=[("ss", s_g)])
                b.emit("vector", lambda e: e.nop(),
                       waits=[("pr", b.val("pr"))])
            b.rel("gb%d" % par, [("vv", v_x)])
            t_st = dma("sync", x_mid[dt_ * 128:(dt_ + 1) * 128,
                                     cc * 256:(cc + 1) * 256],
                       xfb[:, par * 256:(par + 1) * 256], "xo",
                       waits=[("vv", v_x)])
            b.rel("xf%d" % par, [("xo", t_st)])
    b.barrier()
    fa.close()

    b.emit("sync", lambda e: e.nop(), waits=[("xo", b.val("xo"))])
    stage_ln(x_mid, "1")
    b.barrier()

    l1 = contextlib.ExitStack()
    xpS = sbuf_in(l1, "xpS", (128, 16 * T), BF16)
    wiA = sbuf_in(l1, "wiA", (128, NDT * 128), BF16)
    wiB = sbuf_in(l1, "wiB", (128, NDT * 128), BF16)
    wBTS = sbuf_in(l1, "wBTS", (128, 16 * DS), BF16)
    bps = sbuf_in(l1, "bps", (128, 256), F32)
    bfm = sbuf_in(l1, "bfm", (16, T), F32)
    d09 = sbuf_in(l1, "d09", (16, T), F32)
    hsS = sbuf_in(l1, "hsS", (16, T), F32)
    wCTS = sbuf_in(l1, "wCTS", (16, DI), F32)
    y2S = sbuf_in(l1, "y2S", (128, 16 * 512), BF16)
    szb = sbuf_in(l1, "szb", (128, 2 * 512), BF16)
    ybf = sbuf_in(l1, "ybf", (128, 512), F32)
    sz_out = sbuf_in(l1, "sz_out", (128, 2 * 2048), BF16)
    wo1A = sbuf_in(l1, "wo1A", (128, 2 * 16 * 128), BF16)
    sout1 = sbuf_in(l1, "sout1", (128, 2 * 512), F32)
    t_wb = dma("gpsimd", wBTS[:, :].rearrange("p (c f) -> p c f", c=16),
               wBT_d[:, :].rearrange("(c p) f -> p c f", p=128), "gd")
    t_wc = dma("sync", wCTS[:, :], wCT_d[:, :])
    for r in range(32):
        wbuf = wiA if r % 2 == 0 else wiB
        t_wi = dma("gpsimd", wbuf[:, :].rearrange("p (c f) -> p c f", c=NDT),
                   winT_d[:, r * 128:(r + 1) * 128].rearrange(
                       "(c p) f -> p c f", p=128), "gd", res="wi%d" % (r % 2))
        for c in range(NJC):
            bk = "A" if c % 2 == 0 else "B"
            po = pbanks[bk]
            for dt_ in range(NDT):
                t_mm = mm(po[:, :], wbuf[:, dt_ * 128:(dt_ + 1) * 128],
                          xnS[:, dt_ * T + c * 512: dt_ * T + (c + 1) * 512],
                          dt_ == 0, dt_ == NDT - 1,
                          waits=[("gd", t_wi)] if dt_ == 0 else [], bank=bk)
            if r < 16:
                s_x = sc(lambda e, r=r, c=c, po=po: e.activation(
                    xpS[:, r * T + c * 512: r * T + (c + 1) * 512], po[:, :],
                    AF.Silu, bias=binS[:, r:r + 1]),
                    waits=[("mm", t_mm)], bank=bk)
            else:
                s_z = sc(lambda e, r=r, c=c, po=po: e.activation(
                    sz_out[:, (r % 2) * 2048 + c * 512:
                           (r % 2) * 2048 + (c + 1) * 512], po[:, :],
                    AF.Silu, bias=binS[:, r:r + 1]),
                    waits=[("mm", t_mm)] + (b.take("szo%d" % (r % 2))
                                            if c == 0 else []), bank=bk)
                if c == NJC - 1:
                    t_szd = dma("sync", sz_scr[r - 16, :, :],
                                sz_out[:, (r % 2) * 2048:(r % 2) * 2048 + 2048],
                                "szd", waits=[("ss", s_z)])
                    b.rel("szo%d" % (r % 2), [("szd", t_szd)])
        b.rel("wi%d" % (r % 2), [("mm", b.val("mm"))])
    b.barrier()
    dump("xp", xpS[:, :])
    for tt in range(NTT):
        for r in range(16):
            t_bp = mm(pE[0:128, tt * 16: tt * 16 + 16],
                      xpS[:, r * T + tt * 128: r * T + (tt + 1) * 128],
                      wBTS[:, r * 16:(r + 1) * 16],
                      r == 0, r == 15,
                      waits=[("gd", t_wb)] if (tt == 0 and r == 0) else [],
                      bank="E" if (tt == 0 and r == 0) else None)
    t_bev = sc(lambda e: e.activation(bps[:, :], pE[0:128, 0:256], AF.Identity,
                                      scale=0.1), waits=[("mm", t_bp)],
               bank="E")
    for tt in range(NTT):
        bk = "F" if tt % 2 == 0 else "G"
        po = pbanks[bk]
        t_tr = trans(po[0:16, 0:128], bps[:, tt * 16: tt * 16 + 16],
                     waits=[("ss", t_bev)] + b.take("bk" + bk))
        ev(lambda e, tt=tt, po=po: e.tensor_copy(
            bfm[:, tt * 128:(tt + 1) * 128], po[0:16, 0:128]),
           waits=[("mm", t_tr)], bank=bk, dr=True)
    t_bfm = b.val("vv")
    ev(lambda e: e.memset(d09[:, :], 0.9), dr=True)
    t_scan = ev(lambda e: e.tensor_tensor_scan(hsS[:, :], d09[:, :], bfm[:, :],
                                               0.0, ALU.mult, ALU.add),
                waits=[("vv", t_bfm)])
    b.barrier()
    dump("bfm", bfm[:, :])
    dump("hs", hsS[:, :])
    for jc in range(NJC):
        for ec in range(16):
            bk = "A" if ec % 2 == 0 else "B"
            po = pbanks[bk]
            t_c = mm(po[:, :], wCTS[:, ec * 128:(ec + 1) * 128],
                     hsS[:, jc * 512:(jc + 1) * 512], True, True,
                     waits=[("vv", t_scan), ("sd", t_wc)], bank=bk)
            v_y = ev(lambda e, ec=ec, jc=jc, po=po: e.scalar_tensor_tensor(
                ybf[:, :], xpS[:, ec * T + jc * 512: ec * T + (jc + 1) * 512],
                dvecS[:, ec:ec + 1], po[:, :], ALU.mult, ALU.add),
                waits=[("mm", t_c)] + b.take("ybf"), bank=bk, dr=True)
            par = ec % 2
            t_sz = dma("gpsimd", szb[:, par * 512:(par + 1) * 512],
                       sz_scr[ec, :, jc * 512:(jc + 1) * 512], "gd",
                       res="szb%d" % par,
                       waits=[("szd", b.val("szd"))])
            v_y2 = ev(lambda e, ec=ec, par=par: e.tensor_tensor(
                y2S[:, ec * 512:(ec + 1) * 512], ybf[:, :],
                szb[:, par * 512:(par + 1) * 512], ALU.mult),
                waits=[("vv", v_y), ("gd", t_sz)] +
                (b.take("y2S") if ec == 0 else []))
            b.rel("ybf", [("vv", v_y2)])
            b.rel("szb%d" % par, [("vv", v_y2)])
        t_y2 = b.val("vv")
        for dt_ in range(NDT):
            par = dt_ % 2
            bk = "C" if par == 0 else "D"
            po = pbanks[bk]
            wobuf = wo1A[:, par * 2048:(par + 1) * 2048]
            t_wo1 = dma("gpsimd",
                        wobuf.rearrange("p (c f) -> p c f", c=16),
                        wout1T_d[:, dt_ * 128:(dt_ + 1) * 128].rearrange(
                            "(c p) f -> p c f", p=128), "gd",
                        res="wo1%d" % par)
            for ec in range(16):
                t_mo = mm(po[:, :], wobuf[:, ec * 128:(ec + 1) * 128],
                          y2S[:, ec * 512:(ec + 1) * 512], ec == 0, ec == 15,
                          waits=[("gd", t_wo1), ("vv", t_y2)]
                          if ec == 0 else [], bank=bk)
            b.rel("wo1%d" % par, [("mm", t_mo)])
            s_o = sc(lambda e, dt_=dt_, po=po, par=par: e.activation(
                sout1[:, par * 512:(par + 1) * 512], po[:, :], AF.Identity,
                bias=bout1S[:, dt_:dt_ + 1]),
                waits=[("mm", t_mo)] + b.take("so1%d" % par), bank=bk)
            t_sd = dma("sync", s_scr[dt_ * 128:(dt_ + 1) * 128,
                                     jc * 512:(jc + 1) * 512],
                       sout1[:, par * 512:(par + 1) * 512], "so",
                       waits=[("ss", s_o)])
            b.rel("so1%d" % par, [("so", t_sd)])
        b.rel("y2S", [("mm", b.val("mm"))])
    b.barrier()
    l1.close()

    cwS = sbuf_in(top, "cwS", (64, T), BF16)
    ra = contextlib.ExitStack()
    ctS = sbuf_in(ra, "ctS", (128, NDT * 64), BF16)
    affS = sbuf_in(ra, "affS", (128, NTT * 64), F32)
    biaS = sbuf_in(ra, "biaS", (128, NTT * 64), F32)
    ebS = sbuf_in(ra, "ebS", (1, 64), F32)
    ebbS = sbuf_in(ra, "ebbS", (128, 64), F32)
    mS = sbuf_in(ra, "mS", (128, NTT), F32)
    ohS = sbuf_in(ra, "ohS", (128, NTT * 64), F32)
    sgS = sbuf_in(ra, "sgS", (128, NTT * 64), F32)
    wsumS = sbuf_in(ra, "wsumS", (128, 2 * NTT), F32)
    cwtS = sbuf_in(ra, "cwtS", (128, NTT * 64), F32)
    t_ct = dma("gpsimd", ctS[:, :].rearrange("p (c f) -> p c f", c=NDT),
               centT_d[:, :].rearrange("(c p) f -> p c f", p=128), "gd")
    t_eb = dma("sync", ebS[0:1, :], ebias_d[:, :])
    t_ebb = mm(pE[0:128, 0:64], onerfS[:, :], ebS[0:1, :], True, True,
               waits=[("sd", t_eb)], bank="E")
    t_ebbe = ev(lambda e: e.tensor_copy(ebbS[:, :], pE[0:128, 0:64]),
                waits=[("mm", t_ebb)], bank="E")
    for tt in range(NTT):
        bk = "A" if tt % 2 == 0 else "B"
        po = pbanks[bk]
        for dt_ in range(NDT):
            t_af = mm(po[0:128, 0:64], xnS[:, dt_ * T + tt * 128:
                                           dt_ * T + (tt + 1) * 128],
                      ctS[:, dt_ * 64:(dt_ + 1) * 64], dt_ == 0,
                      dt_ == NDT - 1,
                      waits=[("gd", t_ct)] if (tt == 0 and dt_ == 0) else [],
                      bank=bk)
        ev(lambda e, tt=tt, po=po: e.tensor_copy(
            affS[:, tt * 64:(tt + 1) * 64], po[0:128, 0:64]),
           waits=[("mm", t_af)], bank=bk, dr=True)
    aff3 = biaS[:, :].rearrange("p (a e) -> p a e", e=64)
    t_bia = ev(lambda e: e.tensor_tensor(
        aff3, affS[:, :].rearrange("p (a e) -> p a e", e=64),
        ebbS[:, :].unsqueeze(1).broadcast_to((128, NTT, 64)), ALU.add),
        waits=[("vv", b.val("vv")), ("vv", t_ebbe)], dr=True)
    for it in range(8):
        t_m = ev(lambda e: e.reduce_max(
            mS[:, :], biaS[:, :].rearrange("p (a e) -> p a e", e=64),
            axis=AX.X), dr=True)
        t_oh = ev(lambda e: e.tensor_tensor(
            ohS[:, :].rearrange("p (a e) -> p a e", e=64),
            biaS[:, :].rearrange("p (a e) -> p a e", e=64),
            mS[:, :].unsqueeze(2).broadcast_to((128, NTT, 64)), ALU.is_equal),
            dr=True)
        t_up = ev(lambda e: e.scalar_tensor_tensor(
            biaS[:, :], ohS[:, :], -2.0e30, biaS[:, :], ALU.mult, ALU.add),
            dr=True)
    t_sel = ev(lambda e: e.tensor_scalar(ohS[:, :], biaS[:, :], -1.0e29, None,
                                         ALU.is_lt), dr=True)
    t_sg = sc(lambda e: e.activation(sgS[:, :], affS[:, :], AF.Sigmoid),
              waits=[("vv", t_bia)])
    t_cw = ev(lambda e: e.tensor_tensor(cwtS[:, :], sgS[:, :], ohS[:, :],
                                        ALU.mult),
              waits=[("vv", t_sel), ("ss", t_sg)], dr=True)
    t_ws = ev(lambda e: e.reduce_sum(
        wsumS[:, 0:NTT], cwtS[:, :].rearrange("p (a e) -> p a e", e=64),
        axis=AX.X), dr=True)
    t_ws2 = ev(lambda e: e.tensor_scalar_add(wsumS[:, 0:NTT], wsumS[:, 0:NTT],
                                             1e-8), dr=True)
    t_rc = ev(lambda e: e.reciprocal(wsumS[:, NTT:2 * NTT], wsumS[:, 0:NTT]),
              dr=True)
    t_cwn = ev(lambda e: e.tensor_tensor(
        cwtS[:, :].rearrange("p (a e) -> p a e", e=64),
        cwtS[:, :].rearrange("p (a e) -> p a e", e=64),
        wsumS[:, NTT:2 * NTT].unsqueeze(2).broadcast_to((128, NTT, 64)),
        ALU.mult))
    dump("aff", affS[:, :])
    dump("cwt", cwtS[:, :])
    for tt in range(NTT):
        bk = "F" if tt % 2 == 0 else "G"
        po = pbanks[bk]
        t_tr = trans(po[0:64, 0:128], cwtS[:, tt * 64:(tt + 1) * 64],
                     waits=[("vv", t_cwn)] + b.take("bk" + bk))
        ev(lambda e, tt=tt, po=po: e.tensor_copy(
            cwS[:, tt * 128:(tt + 1) * 128], po[0:64, 0:128]),
           waits=[("mm", t_tr)], bank=bk)
    b.barrier()
    dump("cw", cwS[:, :])
    ra.close()

    sele_np = np.zeros((64, 64 * 128), dtype=BF)
    for e_ in range(64):
        sele_np[e_, e_ * 128:(e_ + 1) * 128] = 1
    sele_h = inline("seleC", sele_np)
    ma = contextlib.ExitStack()
    seleS = sbuf_in(ma, "seleS", (64, 64 * 128), BF16)
    routedS = sbuf_in(ma, "routedS", (128, NDT * 1024), F32)
    hsX = sbuf_in(ma, "hsX", (128, 16 * 512), BF16)
    wupA = sbuf_in(ma, "wupA", (128, 8 * 512), BF16)
    wupB = sbuf_in(ma, "wupB", (128, 8 * 512), BF16)
    wdnR = sbuf_in(ma, "wdnR", (128, 6 * 4096), BF16)
    htmp = sbuf_in(ma, "htmp", (128, 2 * 512), BF16)
    gS = sbuf_in(ma, "gS", (128, 2 * 512), F32)
    xfb2 = sbuf_in(ma, "xfb2", (128, 2 * 512), F32)
    sfb2 = sbuf_in(ma, "sfb2", (128, 2 * 512), F32)
    dma("sync", seleS[:, :], sele_h[:, :])
    wdn_tok = {}
    for c in range(2):
        sh = contextlib.ExitStack()
        wsupS = sbuf_in(sh, "wsupS%d" % c, (128, 8 * 512), BF16)
        wsdnS = sbuf_in(sh, "wsdnS%d" % c, (128, 4 * 1024), BF16)
        hshS = sbuf_in(sh, "hshS%d" % c, (128, 4 * 1024), BF16)
        t_su = dma("gpsimd", wsupS[:, :].rearrange("p (a f) -> p a f", a=NDT),
                   wsup_d[:, :].rearrange("(a p) f -> p a f", p=128), "gd")
        t_sd2 = dma("gpsimd", wsdnS[:, :].rearrange("p (a f) -> p a f", a=4),
                    wsdn_d[:, :].rearrange("(a p) f -> p a f", p=128), "gd")
        for jc in range(2):
            for fo in range(4):
                bk = "A" if fo % 2 == 0 else "B"
                po = pbanks[bk]
                for dt_ in range(NDT):
                    t_u = mm(po[:, :],
                             wsupS[:, dt_ * 512 + fo * 128:
                                   dt_ * 512 + (fo + 1) * 128],
                             xnS[:, dt_ * T + c * 1024 + jc * 512:
                                 dt_ * T + c * 1024 + (jc + 1) * 512],
                             dt_ == 0, dt_ == NDT - 1,
                             waits=[("gd", t_su)] if (jc == 0 and fo == 0
                                                      and dt_ == 0) else [],
                             bank=bk)
                sc(lambda e, fo=fo, jc=jc, po=po, hshS=hshS: e.activation(
                    hshS[:, fo * 1024 + jc * 512: fo * 1024 + (jc + 1) * 512],
                    po[:, :], AF.Silu), waits=[("mm", t_u)], bank=bk)
            t_hsh = b.val("ss")
            for dt_ in range(NDT):
                bk = "C" if dt_ % 2 == 0 else "D"
                po = pbanks[bk]
                for fc in range(4):
                    t_d = mm(po[:, :],
                             wsdnS[:, fc * 1024 + dt_ * 128:
                                   fc * 1024 + (dt_ + 1) * 128],
                             hshS[:, fc * 1024 + jc * 512:
                                  fc * 1024 + (jc + 1) * 512],
                             fc == 0, fc == 3,
                             waits=[("ss", t_hsh), ("gd", t_sd2)]
                             if fc == 0 else [], bank=bk)
                ev(lambda e, dt_=dt_, jc=jc, po=po: e.tensor_copy(
                    routedS[:, dt_ * 1024 + jc * 512:
                            dt_ * 1024 + (jc + 1) * 512], po[:, :]),
                   waits=[("mm", t_d)], bank=bk)
        t_shared = b.val("vv")
        b.emit("gpsimd", lambda e: e.nop(), waits=[("vv", t_shared)])
        sh.close()
        for grp in range(16):
            for e4 in range(4):
                ee = grp * 4 + e4
                wub = wupA if ee % 2 == 0 else wupB
                t_wu = dma("gpsimd",
                           wub[:, :].rearrange("p (a f) -> p a f", a=NDT),
                           wup_d[ee, :, :].rearrange("(a p) f -> p a f",
                                                     p=128), "gd",
                           res="wup%d" % (ee % 2))
                sl = ee % 6
                wdn_tok[ee] = dma("gpsimd",
                                  wdnR[:, sl * 4096:(sl + 1) * 4096]
                                  .rearrange("p (a f) -> p a f", a=4),
                                  wdn_d[ee, :, :].rearrange(
                                      "(a p) f -> p a f", p=128), "gd",
                                  res="wdn%d" % sl)
                for jc in range(2):
                    cb_bk = "E" if jc == 0 else "F"
                    pcb = pbanks[cb_bk]
                    t_cb = mm(pcb[:, :], seleS[:, ee * 128:(ee + 1) * 128],
                              cwS[:, c * 1024 + jc * 512:
                                  c * 1024 + (jc + 1) * 512],
                              True, True, bank=cb_bk)
                    for fo in range(4):
                        bk = "A" if fo % 2 == 0 else "B"
                        po = pbanks[bk]
                        for dt_ in range(NDT):
                            t_u = mm(po[:, :],
                                     wub[:, dt_ * 512 + fo * 128:
                                         dt_ * 512 + (fo + 1) * 128],
                                     xnS[:, dt_ * T + c * 1024 + jc * 512:
                                         dt_ * T + c * 1024 + (jc + 1) * 512],
                                     dt_ == 0, dt_ == NDT - 1,
                                     waits=[("gd", t_wu)]
                                     if (jc == 0 and fo == 0 and dt_ == 0)
                                     else [], bank=bk)
                        par = fo % 2
                        s_h = sc(lambda e, po=po, par=par: e.activation(
                            htmp[:, par * 512:(par + 1) * 512], po[:, :],
                            AF.Silu),
                            waits=[("mm", t_u)] + b.take("ht%d" % par),
                            bank=bk)
                        v_h = ev(lambda e, e4=e4, fo=fo, par=par, pcb=pcb:
                                 e.tensor_tensor(
                                     hsX[:, (e4 * 4 + fo) * 512:
                                         (e4 * 4 + fo + 1) * 512],
                                     htmp[:, par * 512:(par + 1) * 512],
                                     pcb[:, :], ALU.mult),
                                 waits=[("ss", s_h), ("mm", t_cb)] +
                                 (b.take("hsX") if (e4 == 0 and fo == 0)
                                  else []))
                        b.rel("ht%d" % par, [("vv", v_h)])
                        if e4 == 3 and fo == 3:
                            b.rel("bk" + cb_bk, [("vv", v_h)])
                            b.rel("cw_used", [("vv", v_h)])
                    if e4 == 3:
                        t_h_all = b.val("vv")
                        for dt_ in range(NDT):
                            bk = "C" if dt_ % 2 == 0 else "D"
                            po = pbanks[bk]
                            idx = 0
                            for e4b in range(4):
                                eeb = grp * 4 + e4b
                                slb = eeb % 6
                                for fc in range(4):
                                    t_d = mm(
                                        po[:, :],
                                        wdnR[:, slb * 4096 + fc * 1024 +
                                             dt_ * 128: slb * 4096 +
                                             fc * 1024 + (dt_ + 1) * 128],
                                        hsX[:, (e4b * 4 + fc) * 512:
                                            (e4b * 4 + fc + 1) * 512],
                                        idx == 0, idx == 15,
                                        waits=([("vv", t_h_all),
                                                ("gd", wdn_tok[eeb])]
                                               if idx == 0 else []),
                                        bank=bk)
                                    idx += 1
                            v_r = ev(lambda e, dt_=dt_, jc=jc, c=c, po=po:
                                     e.tensor_tensor(
                                         routedS[:, dt_ * 1024 + jc * 512:
                                                 dt_ * 1024 + (jc + 1) * 512],
                                         routedS[:, dt_ * 1024 + jc * 512:
                                                 dt_ * 1024 + (jc + 1) * 512],
                                         po[:, :], ALU.add),
                                     waits=[("mm", t_d)], bank=bk)
                        if jc == 1:
                            for e4b in range(4):
                                b.rel("wdn%d" % ((grp * 4 + e4b) % 6),
                                      [("vv", v_r)])
                            b.rel("hsX", [("vv", v_r)])
                b.rel("wup%d" % (ee % 2), [("mm", b.val("mm"))])
        t_route_done = b.val("vv")
        if debug and c == 0:
            b.barrier()
            dump("routed0", routedS[:, :])
            b.barrier()
            t_route_done = b.val("vv")
        for dt_ in range(NDT):
            for jc in range(2):
                par = (dt_ * 2 + jc) % 2
                s_g = sc(lambda e, dt_=dt_, jc=jc, par=par: e.activation(
                    gS[:, par * 512:(par + 1) * 512],
                    routedS[:, dt_ * 1024 + jc * 512:
                            dt_ * 1024 + (jc + 1) * 512], AF.Silu),
                    waits=[("vv", t_route_done)] + b.take("gS%d" % par))
                t_lx = dma("sync", xfb2[:, par * 512:(par + 1) * 512],
                           x_mid[dt_ * 128:(dt_ + 1) * 128,
                                 c * 1024 + jc * 512:
                                 c * 1024 + (jc + 1) * 512],
                           res="xf2%d" % par, waits=[("xo", b.val("xo"))])
                t_ls = dma("sync", sfb2[:, par * 512:(par + 1) * 512],
                           s_scr[dt_ * 128:(dt_ + 1) * 128,
                                 c * 1024 + jc * 512:
                                 c * 1024 + (jc + 1) * 512],
                           res="sf2%d" % par, waits=[("so", b.val("so"))])
                v_m = ev(lambda e, par=par: e.tensor_tensor(
                    gS[:, par * 512:(par + 1) * 512],
                    gS[:, par * 512:(par + 1) * 512],
                    sfb2[:, par * 512:(par + 1) * 512], ALU.mult),
                    waits=[("ss", s_g), ("sd", t_ls)], dr=True)
                b.rel("sf2%d" % par, [("vv", v_m)])
                v_x = ev(lambda e, par=par: e.tensor_tensor(
                    xfb2[:, par * 512:(par + 1) * 512],
                    xfb2[:, par * 512:(par + 1) * 512],
                    gS[:, par * 512:(par + 1) * 512], ALU.add),
                    waits=[("sd", t_lx)])
                b.rel("gS%d" % par, [("vv", v_x)])
                t_st = dma("sync", out[dt_ * 128:(dt_ + 1) * 128,
                                       c * 1024 + jc * 512:
                                       c * 1024 + (jc + 1) * 512],
                           xfb2[:, par * 512:(par + 1) * 512], "xo2",
                           waits=[("vv", v_x)])
                b.rel("xf2%d" % par, [("xo2", t_st)])
    b.emit("sync", lambda e: e.nop(),
           waits=[("xo2", b.val("xo2"))] +
           ([("dbg", b.val("dbg"))] if debug else []))
    ma.close()
    top.close()

    sems = {}
    sem_stack = contextlib.ExitStack()
    all_sems = set()
    for eng in ENGINES:
        for fn, waits, incs in b.ops[eng]:
            for s, v in waits:
                all_sems.add(s)
            for s, a in incs:
                all_sems.add(s)
    for s in sorted(all_sems):
        sems[s] = sem_stack.enter_context(nc.semaphore(s))
    with nc.Block() as block:
        def run_engine(eng_obj, eng_name):
            for fn, waits, incs in b.ops[eng_name]:
                for s, v in waits:
                    eng_obj.wait_ge(sems[s], v)
                inst = fn(eng_obj)
                for s, a in incs:
                    inst.then_inc(sems[s], a)
        block.sync(lambda e: run_engine(e, "sync"))
        block.gpsimd(lambda e: run_engine(e, "gpsimd"))
        block.tensor(lambda e: run_engine(e, "tensor"))
        block.vector(lambda e: run_engine(e, "vector"))
        block.scalar(lambda e: run_engine(e, "scalar"))
    sem_stack.close()
    return nc


def host_inputs(inputs):
    f32 = np.float32
    x = np.asarray(inputs["L0_attn_x"], f32)
    per_core = []
    shared = {
        "wqkvT": np.ascontiguousarray(np.asarray(inputs["L0_attn_w_qkv"], f32).T),
        "bqkv": np.ascontiguousarray(
            np.asarray(inputs["L0_attn_b_qkv"], f32).reshape(3 * NDT, 128).T),
        "bvrow": np.ascontiguousarray(
            np.asarray(inputs["L0_attn_b_qkv"], f32).reshape(1, 3 * D)[:, 2 * D:]),
        "wlinesT": np.ascontiguousarray(np.concatenate(
            [np.asarray(inputs["L0_attn_w1_write"], f32).T,
             np.asarray(inputs["L0_attn_w2_write"], f32).T,
             np.asarray(inputs["L0_attn_w1_read"], f32).T,
             np.asarray(inputs["L0_attn_w2_read"], f32).T], axis=1)),
        "woutT": np.ascontiguousarray(np.asarray(inputs["L0_attn_w_out"], f32).T),
        "bout0": np.ascontiguousarray(
            np.asarray(inputs["L0_attn_b_out"], f32).reshape(NDT, 128).T),
        "dlog": np.ascontiguousarray(
            np.asarray(inputs["L0_attn_decay_logits"], f32).reshape(H, 1)),
        "bscale": np.ascontiguousarray(
            np.asarray(inputs["L0_attn_bias_scale"], f32).reshape(H, 1)),
        "w1T": np.ascontiguousarray(np.asarray(inputs["L0_ffn_W1"], f32).T),
        "b1v": np.ascontiguousarray(
            np.asarray(inputs["L0_ffn_b1"], f32).reshape(FF // 128, 128).T),
        "w2T": np.ascontiguousarray(np.asarray(inputs["L0_ffn_W2"], f32).T),
        "b2v": np.ascontiguousarray(
            np.asarray(inputs["L0_ffn_b2"], f32).reshape(NDT, 128).T),
        "winT": np.ascontiguousarray(np.asarray(inputs["L1_attn_W_in"], f32).T),
        "binv": np.ascontiguousarray(
            np.asarray(inputs["L1_attn_b_in"], f32).reshape(2 * DI // 128, 128).T),
        "wBT": np.ascontiguousarray(np.asarray(inputs["L1_attn_W_B"], f32).T),
        "wCT": np.ascontiguousarray(np.asarray(inputs["L1_attn_W_C"], f32).T),
        "dvec": np.ascontiguousarray(
            np.asarray(inputs["L1_attn_D"], f32).reshape(DI // 128, 128).T),
        "wout1T": np.ascontiguousarray(
            np.asarray(inputs["L1_attn_W_out"], f32).T),
        "bout1": np.ascontiguousarray(
            np.asarray(inputs["L1_attn_b_out"], f32).reshape(NDT, 128).T),
        "centT": np.ascontiguousarray(
            np.asarray(inputs["L1_ffn_centroids"], f32).T),
        "ebias": np.ascontiguousarray(
            np.asarray(inputs["L1_ffn_expert_biases"], f32).reshape(1, NE)),
        "wsup": np.ascontiguousarray(np.asarray(inputs["L1_ffn_Ws_up"], f32)),
        "wsdn": np.ascontiguousarray(np.asarray(inputs["L1_ffn_Ws_down"], f32)),
        "wup": np.ascontiguousarray(np.asarray(inputs["L1_ffn_W_up"], f32)),
        "wdn": np.ascontiguousarray(np.asarray(inputs["L1_ffn_W_down"], f32)),
    }
    for core in range(8):
        m = dict(shared)
        m["x"] = np.ascontiguousarray(x[core].T)
        per_core.append(m)
    return per_core


_NC_CACHE = {}


def kernel(**inputs):
    if "prog" not in _NC_CACHE:
        _NC_CACHE["prog"] = build_program()
    nc = _NC_CACHE["prog"]
    in_maps = host_inputs(inputs)
    res = run_bass_kernel_spmd(nc, in_maps, core_ids=list(range(8)))
    outs = [r["out"].T for r in res.results]
    return np.ascontiguousarray(np.stack(outs, axis=0)).astype(np.float32)


